# version 06
# baseline (speedup 1.0000x reference)
import numpy as np
import ml_dtypes
from contextlib import ExitStack

import concourse.bass as bass
import concourse.mybir as mybir
from concourse.bass_utils import run_bass_kernel_spmd

F32 = mybir.dt.float32
BF16 = mybir.dt.bfloat16
AF = mybir.ActivationFunctionType
ALU = mybir.AluOpType
AX = mybir.AxisListType

D = 2048
DC = 16
H = 16
R = 16
SCALE = 128.0 ** -0.5
T = 1024
TF = 2048
E = 16
IC = 11
SIC = 22
EPS = 1e-5
NTT = T // 512


class Ctr:
    def __init__(self, nc, es, name):
        self.sem = es.enter_context(nc.semaphore(name))
        self.n = 0
        self._seen = {}

    def inc(self, instr, k=1):
        instr.then_inc(self.sem, k)
        self.n += k
        return self.n

    def wait(self, eng, val):
        if val <= 0:
            return
        key = id(eng)
        if self._seen.get(key, -1) >= val:
            return
        eng.wait_ge(self.sem, val)
        self._seen[key] = val


class Qd:

    def __init__(self, nc, es, eng, name, lanes=12):
        self.eng = eng
        self.sems = [es.enter_context(nc.semaphore(f"{name}{i}")) for i in range(lanes)]
        self.counts = [0] * lanes
        self.i = 0
        self._seen = {}

    def dma(self, out, in_):
        ln = self.i % len(self.sems)
        self.i += 1
        self.eng.dma_start(out=out, in_=in_).then_inc(self.sems[ln], 16)
        self.counts[ln] += 16
        return (ln, self.counts[ln])

    def wait(self, eng, tok):
        if not tok:
            return
        ln, v = tok
        key = (id(eng), ln)
        if self._seen.get(key, -1) >= v:
            return
        eng.wait_ge(self.sems[ln], v)
        self._seen[key] = v

    def snapshot(self):
        return list(self.counts)

    def wait_snap(self, eng, snap):
        for ln, v in enumerate(snap):
            if v:
                self.wait(eng, (ln, v))


def build(debug=False):
    nc = bass.Bass("TRN2", target_bir_lowering=False)
    dram = {}

    def din(name, shape, dt=F32):
        dram[name] = nc.dram_tensor(name, list(shape), dt, kind="ExternalInput").ap()
        return dram[name]

    d_xT = din("xT", [128, DC, TF])
    d_xo = din("xTo", [128, DC, T])
    d_wq = din("wqT", [H, 128, DC, 128])
    d_wk = din("wkT", [H, 128, DC, 128])
    d_wv = din("wvT", [H, 128, DC, 128])
    d_wo = din("woT", [4, 128, DC, 512])
    d_bq = din("bq", [128, H])
    d_bk = din("bk", [128, H])
    d_bv = din("bv", [128, H])
    d_bo = din("bo", [128, H])
    d_Bqkv = din("BqkvT", [128, DC, 3 * R])
    d_Bo = din("BoT", [128, DC, R])
    d_Aq = din("AqT", [R, D])
    d_Ak = din("AkT", [R, D])
    d_Av = din("AvT", [R, D])
    d_Ao = din("AoT", [R, D])
    d_gw = din("gateT", [128, DC, E])
    d_up = din("upT", [E, IC, 128, DC, 256])
    d_dn = din("dnT", [E, 8, 128, IC, 256])
    d_sup = din("supT", [2, IC, 128, DC, 256])
    d_sdn = din("sdnT", [2, 8, 128, IC, 256])
    d_id = din("identity", [128, 128])
    d_ones = din("ones", [128, 128])
    d_esel = din("esel", [R, E * 128])
    d_eps = din("epsv", [128, 1])

    d_out = nc.dram_tensor("out", [T, D], F32, kind="ExternalOutput").ap()
    d_hsp = nc.dram_tensor("hspill", [DC, 128, T], F32, kind="Internal").ap()

    dbg = {}
    if debug:
        for nm in ("dbg_resid1", "dbg_h", "dbg_acc"):
            dbg[nm] = nc.dram_tensor(nm, [DC, 128, T], F32, kind="ExternalOutput").ap()
        dbg["dbg_cw"] = nc.dram_tensor("dbg_cw", [R, T], F32, kind="ExternalOutput").ap()
        dbg["dbg_xb"] = nc.dram_tensor("dbg_xb", [3, R, TF], BF16, kind="ExternalOutput").ap()
        for nm in ("dbg_k0", "dbg_v0", "dbg_vt0"):
            dbg[nm] = nc.dram_tensor(nm, [128, TF], BF16, kind="ExternalOutput").ap()
        dbg["dbg_q0"] = nc.dram_tensor("dbg_q0", [128, T], BF16, kind="ExternalOutput").ap()
        dbg["dbg_ao"] = nc.dram_tensor("dbg_ao", [128, H, T], BF16, kind="ExternalOutput").ap()
        dbg["dbg_at0"] = nc.dram_tensor("dbg_at0", [128, DC, 512], BF16, kind="ExternalOutput").ap()
        dbg["dbg_exp0"] = nc.dram_tensor("dbg_exp0", [128, TF], BF16, kind="ExternalOutput").ap()
        dbg["dbg_sm0"] = nc.dram_tensor("dbg_sm0", [128, 8], F32, kind="ExternalOutput").ap()
        dbg["dbg_dg0"] = nc.dram_tensor("dbg_dg0", [128, 128], BF16, kind="ExternalOutput").ap()

    es0 = ExitStack()
    pe = Ctr(nc, es0, "pe")
    dv = Ctr(nc, es0, "dv")
    ac = Ctr(nc, es0, "ac")
    gp = Ctr(nc, es0, "gp")
    gq = Qd(nc, es0, nc.gpsimd, "gq")
    sq = Qd(nc, es0, nc.sync, "sq")
    so = es0.enter_context(nc.semaphore("so"))
    so_n = [0]

    def barrier_tokens():
        return (pe.n, dv.n, ac.n, gq.snapshot(), sq.snapshot())

    def guard(eng, bar):
        pe.wait(eng, bar[0]); dv.wait(eng, bar[1]); ac.wait(eng, bar[2])
        gq.wait_snap(eng, bar[3]); sq.wait_snap(eng, bar[4])

    PS0 = es0.enter_context(nc.psum_tensor("PS0", [128, 2048], F32))
    PS1 = es0.enter_context(nc.psum_tensor("PS1", [128, 512], F32))
    PS2 = es0.enter_context(nc.psum_tensor("PS2", [128, 512], F32))
    PS3 = es0.enter_context(nc.psum_tensor("PS3", [128, 512], F32))
    PS4 = es0.enter_context(nc.psum_tensor("PS4", [128, 512], F32))

    bank = {id(PS1): (ac, 0), id(PS2): (ac, 0), id(PS3): (ac, 0), id(PS4): (ac, 0)}

    def bk_wait(ps):
        c, v = bank[id(ps)]
        c.wait(nc.tensor, v)

    def bk_set(ps, c, v):
        bank[id(ps)] = (c, v)

    sb0 = lambda name, shape, dt: es0.enter_context(nc.sbuf_tensor(name, list(shape), dt))
    ident_bf = sb0("ident_bf", [128, 128], BF16)
    ident_f = sb0("ident_f", [128, 128], F32)
    ones_f = sb0("ones_f", [128, 128], F32)
    bq_sb = sb0("bq_sb", [128, H], F32)
    bk_sb = sb0("bk_sb", [128, H], F32)
    bv_sb = sb0("bv_sb", [128, H], F32)
    bo_sb = sb0("bo_sb", [128, H], F32)
    gw_sb = sb0("gw_sb", [128, DC, E], F32)
    small = sb0("small", [128, 8], F32)
    eps_sb = sb0("eps_sb", [128, 1], F32)

    tok_identbf = gq.dma(ident_bf[:, :], d_id)
    sq.dma(ident_f[:, :], d_id)
    sq.dma(ones_f[:, :], d_ones)
    sq.dma(bq_sb[:, :], d_bq)
    sq.dma(bk_sb[:, :], d_bk)
    sq.dma(bv_sb[:, :], d_bv)
    sq.dma(bo_sb[:, :], d_bo)
    sq.dma(gw_sb[:, :, :], d_gw)
    sq.dma(eps_sb[:, :], d_eps)
    sq_const = sq.snapshot()
    bqs_sb = sb0("bqs_sb", [128, H], F32)
    sq.wait_snap(nc.vector, sq_const)
    di = nc.vector.tensor_scalar_mul(bqs_sb[:, :], bq_sb[:, :], SCALE)
    t_bqs = dv.inc(di)

    esAO = ExitStack()
    attn_out = esAO.enter_context(nc.sbuf_tensor("attn_out", [128, H, T], BF16))
    esA = ExitStack()
    sbA = lambda name, shape, dt: esA.enter_context(nc.sbuf_tensor(name, list(shape), dt))
    x_fm = sbA("x_fm", [128, DC, TF], BF16)
    K_sb = sbA("K_sb", [128, TF], BF16)
    Vf_sb = sbA("Vf_sb", [128, TF], BF16)
    Vt_sb = sbA("Vt_sb", [128, DC, 128], BF16)
    Q_sb = sbA("Q_sb", [128, T], BF16)
    attn_sb = sbA("attn_sb", [128, 2, TF], BF16)
    attnT = sbA("attnT", [128, DC, 512], BF16)
    diag_bf = sbA("diag_bf", [128, 2, 128], BF16)
    wring = sbA("wring", [128, 2, 3, DC, 128], BF16)
    Bq_sb = sbA("Bq_sb", [128, DC, 3 * R], BF16)
    Aq_sb = sbA("Aq_sb", [R, D], BF16)
    Ak_sb = sbA("Ak_sb", [R, D], BF16)
    Av_sb = sbA("Av_sb", [R, D], BF16)
    dve_junk = sbA("dve_junk", [128, 512], BF16)
    xBq = sbA("xBq", [R, T], BF16)
    xBk = sbA("xBk", [R, TF], BF16)
    xBv = sbA("xBv", [R, TF], BF16)

    tok_x = gq.dma(x_fm[:, :, :], d_xT)
    tok_Bq = gq.dma(Bq_sb[:, :, :], d_Bqkv)
    tok_Aq = gq.dma(Aq_sb[:, :], d_Aq)
    tok_Ak = gq.dma(Ak_sb[:, :], d_Ak)
    tok_Av = gq.dma(Av_sb[:, :], d_Av)

    head_proj_tok = [0] * H
    head_dma_tok = [None] * H

    def head_dma(h):
        if h >= 2:
            pe.wait(nc.gpsimd, head_proj_tok[h - 2])
        t1 = gq.dma(wring[:, h % 2, 0, :, :], d_wk[h])
        t2 = gq.dma(wring[:, h % 2, 1, :, :], d_wv[h])
        t3 = gq.dma(wring[:, h % 2, 2, :, :], d_wq[h])
        head_dma_tok[h] = (t1, t2, t3)

    head_dma(0)

    gq.wait(nc.tensor, tok_x)
    gq.wait(nc.tensor, tok_Bq)
    for tt in range(4):
        combos = [(PS2, R, xBk), (PS3, 2 * R, xBv)]
        if tt < NTT:
            combos.append((PS1, 0, xBq))
        for (ps, off, xB) in combos:
            for dc in range(DC):
                pe_i = nc.tensor.matmul(
                    ps[0:R, 0:512],
                    Bq_sb[:, dc, off:off + R],
                    x_fm[:, dc, tt * 512:(tt + 1) * 512],
                    start=(dc == 0), stop=(dc == DC - 1))
            t_xb = pe.inc(pe_i)
            pe.wait(nc.scalar, t_xb)
            ai = nc.scalar.activation(out=xB[:, tt * 512:(tt + 1) * 512],
                                      in_=ps[0:R, 0:512], func=AF.Identity, scale=0.5)
            ac.inc(ai)
    ac_xb = ac.n
    for ps_ in (PS1, PS2, PS3):
        bk_set(ps_, ac, ac_xb)
    if debug:
        ac.wait(nc.sync, ac_xb)
        nc.sync.dma_start(out=dbg["dbg_xb"][0][:, 0:T], in_=xBq[:, :]).then_inc(so, 16)
        nc.sync.dma_start(out=dbg["dbg_xb"][1][:, 0:TF], in_=xBk[:, :]).then_inc(so, 16)
        nc.sync.dma_start(out=dbg["dbg_xb"][2][:, 0:TF], in_=xBv[:, :]).then_inc(so, 16)
        so_n[0] += 48
    sq.wait_snap(nc.scalar, sq_const)

    exp_tok = 0
    avevac_tok = 0

    for h in range(H):
        if h + 1 < H:
            head_dma(h + 1)
        slot = h % 2
        for t_ in head_dma_tok[h]:
            gq.wait(nc.tensor, t_)
        gq.wait(nc.tensor, tok_Aq)
        gq.wait(nc.tensor, tok_Ak)
        gq.wait(nc.tensor, tok_Av)
        gq.wait(nc.tensor, tok_identbf)
        gq.wait(nc.vector, tok_identbf)
        ac.wait(nc.tensor, ac_xb)

        pair_jobs = [("k", 0, 1), ("k", 2, 3), ("v", 0, 1), ("v", 2, 3),
                     ("q", 0, 1)]
        evac_tok = {}
        for j, (kind, ta, tb) in enumerate(pair_jobs):
            bA, bB = ((PS3, PS4), (PS1, PS2))[j % 2]
            bk_wait(bA)
            bk_wait(bB)
            widx = {"k": 0, "v": 1, "q": 2}[kind]
            for dc in range(DC):
                nc.tensor.matmul(bA[:, 0:512], wring[:, slot, widx, dc, :],
                                 x_fm[:, dc, ta * 512:(ta + 1) * 512],
                                 start=(dc == 0), stop=False)
                nc.tensor.matmul(bB[:, 0:512], wring[:, slot, widx, dc, :],
                                 x_fm[:, dc, tb * 512:(tb + 1) * 512],
                                 start=(dc == 0), stop=False)
            A_sb, xB = {"k": (Ak_sb, xBk), "v": (Av_sb, xBv), "q": (Aq_sb, xBq)}[kind]
            nc.tensor.matmul(bA[:, 0:512], A_sb[:, h * 128:(h + 1) * 128],
                             xB[:, ta * 512:(ta + 1) * 512], start=False, stop=True)
            pe_i = nc.tensor.matmul(bB[:, 0:512], A_sb[:, h * 128:(h + 1) * 128],
                                    xB[:, tb * 512:(tb + 1) * 512], start=False, stop=True)
            tok = pe.inc(pe_i)
            pe.wait(nc.scalar, tok)
            dst = {"k": K_sb, "v": Vf_sb, "q": Q_sb}[kind]
            bsb = {"k": bk_sb, "v": bv_sb, "q": bqs_sb}[kind]
            scl = SCALE if kind == "q" else 1.0
            if kind == "q":
                dv.wait(nc.scalar, t_bqs)
            for ps_, tt in ((bA, ta), (bB, tb)):
                ai = nc.scalar.activation(out=dst[:, tt * 512:(tt + 1) * 512],
                                          in_=ps_[:, 0:512], func=AF.Identity,
                                          bias=bsb[:, h:h + 1], scale=scl)
                evac_tok[(kind, tt)] = ac.inc(ai)
                bk_set(ps_, ac, evac_tok[(kind, tt)])

        ac.wait(nc.tensor, evac_tok[("v", 2)])
        ac.wait(nc.tensor, evac_tok[("v", 3)])
        for r in range(4):
            psv = (PS1, PS2)[r % 2]
            bk_wait(psv)
            for j in range(4):
                tc = 4 * r + j
                pe_i = nc.tensor.matmul(
                    psv[:, j * 128:(j + 1) * 128],
                    Vf_sb[:, tc * 128:(tc + 1) * 128],
                    ident_bf[:, :], start=True, stop=True)
            tok = pe.inc(pe_i)
            pe.wait(nc.vector, tok)
            di = nc.vector.tensor_copy(Vt_sb[:, 4 * r:4 * r + 4, :], psv[:, 0:512])
            bk_set(psv, dv, dv.inc(di))
        head_proj_tok[h] = pe.n
        if debug and h == 0:
            ac.wait(nc.sync, ac.n)
            dv.wait(nc.sync, dv.n)
            nc.sync.dma_start(out=dbg["dbg_k0"], in_=K_sb[:, :]).then_inc(so, 16)
            nc.sync.dma_start(out=dbg["dbg_v0"], in_=Vf_sb[:, :]).then_inc(so, 16)
            nc.sync.dma_start(out=dbg["dbg_vt0"], in_=Vt_sb[:, :, :]).then_inc(so, 16)
            nc.sync.dma_start(out=dbg["dbg_q0"], in_=Q_sb[:, :]).then_inc(so, 16)
            so_n[0] += 64

        nqc = T // 128
        sc_toks = [0] * nqc
        exp_toks = [0] * nqc
        nm_toks = [0] * nqc
        diag_toks = [0] * nqc
        tr_pe_tok = [0] * nqc
        trg_dv = [0, 0]
        trg_ac = [0, 0]
        for it in range(nqc + 1):
            if it < nqc:
                qc = it
                par = qc % 2
                ac.wait(nc.tensor, exp_tok)
                for j in range(4):
                    pe_i = nc.tensor.matmul(
                        PS0[:, j * 512:(j + 1) * 512],
                        Q_sb[:, qc * 128:(qc + 1) * 128],
                        K_sb[:, j * 512:(j + 1) * 512],
                        start=True, stop=True)
                sc_toks[qc] = pe.inc(pe_i)
            if it >= 1:
                qcp = it - 1
                parp = qcp % 2
                ac.wait(nc.vector, exp_toks[qcp])
                di = nc.vector.reciprocal(small[:, 4 + parp:5 + parp],
                                          small[:, 2 + parp:3 + parp])
                dv.inc(di)
            if it < nqc:
                pe.wait(nc.vector, sc_toks[it])
                di = nc.vector.reduce_max(small[:, it % 2:it % 2 + 1],
                                          PS0[:, 0:1024], axis=AX.X, negate=True)
                nm_toks[it] = dv.inc(di)
            elif it >= 1:
                di = nc.vector.tensor_copy(dve_junk[:, :], K_sb[:, 0:512])
                dv.inc(di)
            if it >= 1:
                qcp = it - 1
                parp = qcp % 2
                if qcp >= 2:
                    pe.wait(nc.vector, tr_pe_tok[qcp - 2])
                di = nc.vector.tensor_scalar_mul(diag_bf[:, parp, :], ident_bf[:, :],
                                                 small[:, 4 + parp:5 + parp])
                diag_toks[qcp] = dv.inc(di)
            if it < nqc:
                qc = it
                par = qc % 2
                dv.wait(nc.scalar, nm_toks[qc])
                ai = nc.scalar.activation(
                    out=attn_sb[:, par, :], in_=PS0[:, :], func=AF.Exp,
                    scale=1.0, bias=small[:, par:par + 1],
                    accum_out=small[:, 2 + par:3 + par])
                exp_tok = ac.inc(ai)
                exp_toks[qc] = exp_tok
                if debug and h == 0 and qc == 0:
                    ac.wait(nc.sync, exp_tok)
                    nc.sync.dma_start(out=dbg["dbg_exp0"], in_=attn_sb[:, par, :]).then_inc(so, 16)
                    nc.sync.dma_start(out=dbg["dbg_sm0"], in_=small[:, :]).then_inc(so, 16)
                    so_n[0] += 32
            if it >= 1:
                qcp = it - 1
                parp = qcp % 2
                qj = qcp % 4
                dv.wait(nc.tensor, diag_toks[qcp])
                for r in range(4):
                    ps = (PS3, PS4, PS1, PS3)[r]
                    bk_wait(ps)
                    for j in range(4):
                        kc = 4 * r + j
                        pe_i = nc.tensor.matmul(
                            ps[:, j * 128:(j + 1) * 128],
                            attn_sb[:, parp, kc * 128:(kc + 1) * 128],
                            diag_bf[:, parp, :], start=True, stop=True)
                    tok = pe.inc(pe_i)
                    if r % 2 == 0:
                        pe.wait(nc.vector, tok)
                        ac.wait(nc.vector, avevac_tok)
                        ei = nc.vector.tensor_copy(
                            attnT[:, 4 * r:4 * r + 4, qj * 128:(qj + 1) * 128],
                            ps[:, 0:512])
                        tkn = dv.inc(ei)
                        bk_set(ps, dv, tkn)
                        trg_dv[qcp // 4] = tkn
                    else:
                        pe.wait(nc.scalar, tok)
                        ac.wait(nc.scalar, avevac_tok)
                        ei = nc.scalar.copy(
                            attnT[:, 4 * r:4 * r + 4, qj * 128:(qj + 1) * 128],
                            ps[:, 0:512])
                        tkn = ac.inc(ei)
                        bk_set(ps, ac, tkn)
                        trg_ac[qcp // 4] = tkn
                tr_pe_tok[qcp] = pe.n
            if it in (4, nqc):
                qg = it // 4 - 1
                if debug and h == 0 and qg == 0:
                    dv.wait(nc.sync, trg_dv[0])
                    ac.wait(nc.sync, trg_ac[0])
                    nc.sync.dma_start(out=dbg["dbg_at0"], in_=attnT[:, :, :]).then_inc(so, 16)
                    so_n[0] += 16
                dv.wait(nc.tensor, trg_dv[qg])
                ac.wait(nc.tensor, trg_ac[qg])
                bk_wait(PS2)
                for kc in range(DC):
                    pe_i = nc.tensor.matmul(
                        PS2[:, 0:512], Vt_sb[:, kc, :], attnT[:, kc, :],
                        start=(kc == 0), stop=(kc == DC - 1))
                tok = pe.inc(pe_i)
                pe.wait(nc.scalar, tok)
                ai = nc.scalar.activation(out=attn_out[:, h, qg * 512:(qg + 1) * 512],
                                          in_=PS2[:, 0:512], func=AF.Identity)
                avevac_tok = ac.inc(ai)
                bk_set(PS2, ac, avevac_tok)

    barA = barrier_tokens()
    esA.close()
    esH = ExitStack()
    acc = esH.enter_context(nc.sbuf_tensor("acc", [128, DC, T], F32))
    rows = esH.enter_context(nc.sbuf_tensor("rows", [1, 3072], F32))
    stat_tm = esH.enter_context(nc.sbuf_tensor("stat_tm", [128, 16], F32))
    gacc = esH.enter_context(nc.sbuf_tensor("gacc", [128, 8, E], F32))
    gsb = esH.enter_context(nc.sbuf_tensor("gsb", [128, 8, 64], F32))
    gss = esH.enter_context(nc.sbuf_tensor("gss", [128, 8, 16], F32))
    gsc2 = esH.enter_context(nc.sbuf_tensor("gsc2", [128, 8, 8], F32))
    gnm = esH.enter_context(nc.sbuf_tensor("gnm", [128, 8], F32))
    cw_sb = esH.enter_context(nc.sbuf_tensor("cw_sb", [128, 8, E], F32))
    cwT = esH.enter_context(nc.sbuf_tensor("cwT", [R, T], F32))
    resid = acc
    h_bf = attn_out
    esLN = ExitStack()
    sbL = lambda name, shape, dt: esLN.enter_context(nc.sbuf_tensor(name, list(shape), dt))
    woring = sbL("woring", [128, 2, DC, 512], BF16)
    Ao_sb = sbL("Ao_sb", [R, D], BF16)
    Bo_sb = sbL("Bo_sb", [128, DC, R], BF16)
    xBo = sbL("xBo", [R, T], BF16)
    xtile = sbL("xtile", [128, 2, T], F32)
    sqt = sbL("sqt", [128, 2, T], F32)
    htmp = sbL("htmp", [128, 3, T], F32)

    guard(nc.gpsimd, barA)
    tok_Ao = gq.dma(Ao_sb[:, :], d_Ao)
    tok_Bo = gq.dma(Bo_sb[:, :, :], d_Bo)
    wo_tok = [None] * 4
    wo_used = [0] * 4

    def wo_dma(og):
        if og >= 2:
            pe.wait(nc.gpsimd, wo_used[og - 2])
        wo_tok[og] = gq.dma(woring[:, og % 2, :, :], d_wo[og])

    wo_dma(0)
    wo_dma(1)
    guard(nc.sync, barA)
    guard(nc.tensor, barA)
    guard(nc.vector, barA)
    guard(nc.scalar, barA)

    if debug:
        ac.wait(nc.sync, avevac_tok)
        nc.sync.dma_start(out=dbg["dbg_ao"], in_=attn_out[:, :, :]).then_inc(so, 16)
        so_n[0] += 16

    gq.wait(nc.tensor, tok_Ao)
    gq.wait(nc.tensor, tok_Bo)
    xbo_evac = avevac_tok
    for tt in range(NTT):
        ac.wait(nc.tensor, xbo_evac)
        for cc in range(DC):
            pe_i = nc.tensor.matmul(
                PS2[0:R, 0:512], Bo_sb[:, cc, :],
                attn_out[:, cc, tt * 512:(tt + 1) * 512],
                start=(cc == 0), stop=(cc == DC - 1))
        tok = pe.inc(pe_i)
        pe.wait(nc.scalar, tok)
        ai = nc.scalar.activation(out=xBo[:, tt * 512:(tt + 1) * 512],
                                  in_=PS2[0:R, 0:512], func=AF.Identity, scale=0.5)
        xbo_evac = ac.inc(ai)

    ac.wait(nc.tensor, exp_tok)
    xdma_tok = [0, 0]
    resid_tok = [0] * DC
    sq_tok = [0] * DC
    stats_tok = 0
    for oc in range(DC):
        og = oc // 4
        if oc % 4 == 0 and og + 1 < 4 and og + 1 >= 2:
            wo_dma(og + 1)
        gq.wait(nc.tensor, wo_tok[og])
        sl = oc % 2
        if oc >= 2:
            dv.wait(nc.sync, resid_tok[oc - 2])
        xdma_tok[sl] = sq.dma(xtile[:, sl, :], d_xo[:, oc, :])
        bA, bB = ((PS3, PS4), (PS1, PS2))[oc % 2]
        bk_wait(bA)
        bk_wait(bB)
        for cc in range(DC):
            nc.tensor.matmul(bA[:, 0:512],
                             woring[:, og % 2, cc, (oc % 4) * 128:(oc % 4 + 1) * 128],
                             attn_out[:, cc, 0:512], start=(cc == 0), stop=False)
            nc.tensor.matmul(bB[:, 0:512],
                             woring[:, og % 2, cc, (oc % 4) * 128:(oc % 4 + 1) * 128],
                             attn_out[:, cc, 512:1024], start=(cc == 0), stop=False)
        ac.wait(nc.tensor, xbo_evac)
        nc.tensor.matmul(bA[:, 0:512], Ao_sb[:, oc * 128:(oc + 1) * 128],
                         xBo[:, 0:512], start=False, stop=True)
        pe_i = nc.tensor.matmul(bB[:, 0:512], Ao_sb[:, oc * 128:(oc + 1) * 128],
                                xBo[:, 512:1024], start=False, stop=True)
        tok = pe.inc(pe_i)
        pe.wait(nc.scalar, tok)
        for ps_, tt in ((bA, 0), (bB, 1)):
            ai = nc.scalar.activation(out=resid[:, oc, tt * 512:(tt + 1) * 512],
                                      in_=ps_[:, 0:512], func=AF.Identity,
                                      bias=bo_sb[:, oc:oc + 1])
            t_ev = ac.inc(ai)
            bk_set(ps_, ac, t_ev)
        wo_used[og] = pe.n
        ac.wait(nc.vector, t_ev)
        sq.wait(nc.vector, xdma_tok[sl])
        di = nc.vector.tensor_tensor(out=resid[:, oc, :], in0=xtile[:, sl, :],
                                     in1=resid[:, oc, :], op=ALU.add)
        resid_tok[oc] = dv.inc(di)
        dv.wait(nc.scalar, resid_tok[oc])
        ai = nc.scalar.activation(out=sqt[:, sl, :], in_=resid[:, oc, :], func=AF.Square)
        sq_tok[oc] = ac.inc(ai)
        dv.wait(nc.tensor, resid_tok[oc])
        ac.wait(nc.tensor, sq_tok[oc])
        sq.wait_snap(nc.tensor, sq_const)
        for tt in range(NTT):
            nc.tensor.matmul(PS0[0:1, tt * 512:(tt + 1) * 512], ones_f[:, 0:1],
                             resid[:, oc, tt * 512:(tt + 1) * 512],
                             start=(oc == 0), stop=(oc == DC - 1))
            pe_i = nc.tensor.matmul(PS0[0:1, 1024 + tt * 512:1024 + (tt + 1) * 512],
                                    ones_f[:, 0:1], sqt[:, sl, tt * 512:(tt + 1) * 512],
                                    start=(oc == 0), stop=(oc == DC - 1))
        stats_tok = pe.inc(pe_i)

    if debug:
        for oc in range(DC):
            dv.wait(nc.sync, resid_tok[oc])
            nc.sync.dma_start(out=dbg["dbg_resid1"][oc], in_=resid[:, oc, :]).then_inc(so, 16)
            so_n[0] += 16


    mu = rows[0:1, 0:1024]
    e2 = rows[0:1, 1024:2048]
    rstd = rows[0:1, 2048:3072]
    pe.wait(nc.vector, stats_tok)
    nc.vector.tensor_scalar_mul(mu, PS0[0:1, 0:1024], 1.0 / D)
    nc.vector.tensor_scalar_mul(e2, PS0[0:1, 1024:2048], 1.0 / D)
    nc.vector.tensor_tensor(out=rstd, in0=mu, in1=mu, op=ALU.mult)
    di = nc.vector.tensor_tensor(out=e2, in0=e2, in1=rstd, op=ALU.subtract)
    t_var = dv.inc(di)
    dv.wait(nc.scalar, t_var)
    sq.wait_snap(nc.scalar, sq_const)
    ai = nc.scalar.activation(out=e2, in_=e2, func=AF.Sqrt, bias=eps_sb[0:1, 0:1])
    t_std = ac.inc(ai)
    ac.wait(nc.vector, t_std)
    di = nc.vector.reciprocal(rstd, e2)
    t_rstd = dv.inc(di)
    dv.wait(nc.tensor, t_rstd)
    for j in range(2):
        nc.tensor.matmul(PS0[:, j * 512:(j + 1) * 512], ones_f[0:1, :],
                         mu[0:1, j * 512:(j + 1) * 512], start=True, stop=True)
        pe_i = nc.tensor.matmul(PS0[:, 1024 + j * 512:1024 + (j + 1) * 512],
                                ones_f[0:1, :], rstd[0:1, j * 512:(j + 1) * 512],
                                start=True, stop=True)
    bc_tok = pe.inc(pe_i)
    pe.wait(nc.vector, bc_tok)

    hsp_tok = [0] * DC
    htmp_tok = [0] * DC
    for oc in range(DC):
        sl = oc % 3
        if oc >= 3:
            pe.wait(nc.vector, htmp_tok[oc - 3])
            sq.wait(nc.vector, hsp_tok[oc - 3])
        nc.vector.tensor_tensor(out=htmp[:, sl, :], in0=resid[:, oc, :],
                                in1=PS0[:, 0:1024], op=ALU.subtract)
        di = nc.vector.tensor_tensor(out=htmp[:, sl, :], in0=htmp[:, sl, :],
                                     in1=PS0[:, 1024:2048], op=ALU.mult)
        t_h = dv.inc(di)
        dv.wait(nc.sync, t_h)
        hsp_tok[oc] = sq.dma(d_hsp[oc], htmp[:, sl, :])
        di = nc.vector.tensor_copy(h_bf[:, oc, :], htmp[:, sl, :])
        dv.inc(di)
        dv.wait(nc.tensor, t_h)
        bk_wait(PS1)
        for tc in range(8):
            pe_i = nc.tensor.matmul(PS1[:, tc * 16:(tc + 1) * 16],
                                    htmp[:, sl, tc * 128:(tc + 1) * 128],
                                    gw_sb[:, oc, :], start=True, stop=True)
        t_g = pe.inc(pe_i)
        htmp_tok[oc] = t_g
        pe.wait(nc.vector, t_g)
        if oc == 0:
            di = nc.vector.tensor_copy(gacc[:, :, :], PS1[:, 0:128])
        else:
            di = nc.vector.tensor_tensor(out=gacc[:, :, :], in0=PS1[:, 0:128],
                                         in1=gacc[:, :, :], op=ALU.add)
        bk_set(PS1, dv, dv.inc(di))
    norm_done = dv.n

    if debug:
        for oc in range(DC):
            sq.wait(nc.sync, hsp_tok[oc])
            nc.sync.dma_start(out=dbg["dbg_h"][oc], in_=d_hsp[oc]).then_inc(so, 16)
            so_n[0] += 16

    gate_state = {}

    def gen_gating():
        TC8 = range(8)
        dv.wait(nc.vector, norm_done)
        nc.vector.tensor_copy(gsb[:, 0, 0:64], acc[:, 0, 0:64])
        yield
        nc.vector.tensor_copy(gsb[:, 1, 0:64], acc[:, 0, 64:128])
        yield
        for tc in TC8:
            di = nc.vector.reduce_max(gnm[:, tc:tc + 1], gacc[:, tc, :], axis=AX.X,
                                      negate=True)
            yield
        t_nmg = dv.inc(di)
        dv.wait(nc.scalar, t_nmg)
        for tc in TC8:
            ai = nc.scalar.activation(out=gsb[:, tc, 0:16], in_=gacc[:, tc, :],
                                      func=AF.Exp, bias=gnm[:, tc:tc + 1])
            yield
        t_gexp = ac.inc(ai)
        ac.wait(nc.vector, t_gexp)
        SC = lambda tc: gsb[:, tc, 0:16]
        MSK = lambda tc: gsb[:, tc, 16:32]
        WW = lambda tc: gsb[:, tc, 32:48]
        T16 = lambda tc: gsb[:, tc, 48:64]
        GMAX = lambda tc: gss[:, tc, 0:4]
        GSEL = lambda tc: gss[:, tc, 4:8]
        T1 = lambda tc: gss[:, tc, 8:12]
        T2 = lambda tc: gss[:, tc, 12:16]
        T4 = lambda tc: gsc2[:, tc, 0:4]
        CNT = lambda tc: gsc2[:, tc, 4:5]
        SEL = lambda tc: gsc2[:, tc, 5:6]
        WSUM = lambda tc: gsc2[:, tc, 6:7]
        WINV = lambda tc: gsc2[:, tc, 7:8]
        for g in range(4):
            for tc in TC8:
                nc.vector.tensor_tensor(out=T1(tc)[:, g:g + 1],
                                        in0=SC(tc)[:, 4 * g:4 * g + 1],
                                        in1=SC(tc)[:, 4 * g + 1:4 * g + 2], op=ALU.max)
                yield
        for g in range(4):
            for tc in TC8:
                nc.vector.tensor_tensor(out=T2(tc)[:, g:g + 1],
                                        in0=SC(tc)[:, 4 * g + 2:4 * g + 3],
                                        in1=SC(tc)[:, 4 * g + 3:4 * g + 4], op=ALU.max)
                yield
        for g in range(4):
            for tc in TC8:
                nc.vector.tensor_tensor(out=GMAX(tc)[:, g:g + 1], in0=T1(tc)[:, g:g + 1],
                                        in1=T2(tc)[:, g:g + 1], op=ALU.max)
                yield
        for g in range(4):
            for tc in TC8:
                nc.vector.tensor_scalar(out=T4(tc), in0=GMAX(tc),
                                        scalar1=GMAX(tc)[:, g:g + 1], scalar2=None,
                                        op0=ALU.is_gt)
                yield
            for tc in TC8:
                nc.vector.reduce_sum(CNT(tc), T4(tc), axis=AX.X)
                yield
            for tc in TC8:
                nc.vector.tensor_scalar(out=GSEL(tc)[:, g:g + 1], in0=CNT(tc),
                                        scalar1=1.5, scalar2=None, op0=ALU.is_lt)
                yield
        for g in range(4):
            for tc in TC8:
                nc.vector.tensor_scalar(out=MSK(tc)[:, 4 * g:4 * g + 4],
                                        in0=SC(tc)[:, 4 * g:4 * g + 4],
                                        scalar1=GSEL(tc)[:, g:g + 1], scalar2=None,
                                        op0=ALU.mult)
                yield
        for e in range(E):
            for tc in TC8:
                nc.vector.tensor_scalar(out=T16(tc), in0=MSK(tc),
                                        scalar1=MSK(tc)[:, e:e + 1], scalar2=None,
                                        op0=ALU.is_gt)
                yield
            for tc in TC8:
                nc.vector.reduce_sum(CNT(tc), T16(tc), axis=AX.X)
                yield
            for tc in TC8:
                nc.vector.tensor_scalar(out=SEL(tc), in0=CNT(tc), scalar1=3.5,
                                        scalar2=None, op0=ALU.is_lt)
                yield
            for tc in TC8:
                nc.vector.tensor_tensor(out=WW(tc)[:, e:e + 1], in0=MSK(tc)[:, e:e + 1],
                                        in1=SEL(tc), op=ALU.mult)
                yield
        for tc in TC8:
            nc.vector.reduce_sum(WSUM(tc), WW(tc), axis=AX.X)
            yield
        for tc in TC8:
            nc.vector.tensor_scalar(out=WSUM(tc), in0=WSUM(tc), scalar1=1e-20,
                                    scalar2=None, op0=ALU.add)
            yield
        for tc in TC8:
            nc.vector.reciprocal(WINV(tc), WSUM(tc))
            yield
        for tc in TC8:
            di = nc.vector.tensor_scalar_mul(cw_sb[:, tc, :], WW(tc), WINV(tc))
            yield
        gate_state['t_cw'] = dv.inc(di)


    def emit_cw_tail():
        dv.wait(nc.tensor, gate_state['t_cw'])
        bk_wait(PS3)
        bk_wait(PS4)
        for tc in range(8):
            ps = PS3 if tc < 4 else PS4
            pe_i = nc.tensor.matmul(ps[0:R, (tc % 4) * 128:(tc % 4 + 1) * 128],
                                    cw_sb[:, tc, :], ident_f[:, :], start=True, stop=True)
        cw_tr_done = pe.inc(pe_i)
        pe.wait(nc.vector, cw_tr_done)
        nc.vector.tensor_copy(cwT[:, 0:512], PS3[0:R, 0:512])
        di = nc.vector.tensor_copy(cwT[:, 512:1024], PS4[0:R, 0:512])
        t_cwT = dv.inc(di)
        bk_set(PS3, dv, t_cwT)
        bk_set(PS4, dv, t_cwT)
        gate_state['t_cwT'] = t_cwT



    gate_iter = gen_gating()
    gate_state["debug"] = debug


    barL = barrier_tokens()
    esLN.close()
    esB = ExitStack()
    a_sb = esB.enter_context(nc.sbuf_tensor("a_sb", [128, IC, T], BF16))
    upring = esB.enter_context(nc.sbuf_tensor("upring", [128, 3, DC, 256], BF16))
    dnring = esB.enter_context(nc.sbuf_tensor("dnring", [128, 3, IC, 256], BF16))
    esel_sb = esB.enter_context(nc.sbuf_tensor("esel_sb", [R, E * 128], F32))
    guard(nc.sync, barL)
    esel_tok = sq.dma(esel_sb[:, :], d_esel)

    up_srcs = [d_up[e] for e in range(E)] + [d_sup[0], d_sup[1]]
    dn_srcs = [d_dn[e] for e in range(E)] + [d_sdn[0], d_sdn[1]]
    ve_order = [E, E + 1] + list(range(E))
    up_seq = [(ve, og) for ve in ve_order for og in range(IC)]
    dn_seq = [(ve, og) for ve in ve_order for og in range(8)]
    up_dma_tok = {}
    dn_dma_tok = {}
    up_used = {}
    dn_used = {}
    up_ptr = [0]
    dn_ptr = [0]

    def pump_up():
        i = up_ptr[0]
        if i >= len(up_seq):
            return
        if i < 3:
            guard(nc.gpsimd, barL)
        else:
            pe.wait(nc.gpsimd, up_used.get(i - 3, 0))
        ve, og = up_seq[i]
        up_dma_tok[i] = gq.dma(upring[:, i % 3, :, :], up_srcs[ve][og])
        up_ptr[0] += 1

    def pump_dn():
        i = dn_ptr[0]
        if i >= len(dn_seq):
            return
        if i < 3:
            guard(nc.gpsimd, barL)
        else:
            pe.wait(nc.gpsimd, dn_used.get(i - 3, 0))
        ve, og = dn_seq[i]
        dn_dma_tok[i] = gq.dma(dnring[:, i % 3, :, :], dn_srcs[ve][og])
        dn_ptr[0] += 1

    for _ in range(3):
        pump_up()
    for _ in range(3):
        pump_dn()

    guard(nc.tensor, barL)
    guard(nc.vector, barL)
    guard(nc.scalar, barL)
    dv.wait(nc.tensor, norm_done)
    cwb_guard = [norm_done, norm_done]

    def drain_gate(n):
        for _ in range(n):
            if next(gate_iter, "done") == "done":
                return

    rot = [0]

    def next_pair():
        p = ((PS1, PS2), (PS3, PS4))[rot[0] % 2]
        rot[0] += 1
        return p

    for vi, ve in enumerate(ve_order):
        routed = ve < E
        parb = ve % 2
        base = parb * 1024
        cwb_tok = 0
        if routed:
            if vi == 2:
                drain_gate(10 ** 9)
                emit_cw_tail()
                cwb_guard = [gate_state["t_cwT"], gate_state["t_cwT"]]
            dv.wait(nc.tensor, cwb_guard[parb])
            sq.wait(nc.tensor, esel_tok)
            for j in range(2):
                pe_i = nc.tensor.matmul(PS0[:, base + j * 512:base + (j + 1) * 512],
                                        esel_sb[:, ve * 128:(ve + 1) * 128],
                                        cwT[:, j * 512:(j + 1) * 512],
                                        start=True, stop=True)
            cwb_tok = pe.inc(pe_i)
        silu_tok = {}
        for og in range(IC):
            gi = vi * IC + og
            gq.wait(nc.tensor, up_dma_tok[gi])
            for half in range(2):
                bA, bB = next_pair()
                bk_wait(bA)
                bk_wait(bB)
                for dc in range(DC):
                    nc.tensor.matmul(
                        bA[:, 0:512],
                        upring[:, gi % 3, dc, half * 128:(half + 1) * 128],
                        h_bf[:, dc, 0:512],
                        start=(dc == 0), stop=(dc == DC - 1))
                    pe_i = nc.tensor.matmul(
                        bB[:, 0:512],
                        upring[:, gi % 3, dc, half * 128:(half + 1) * 128],
                        h_bf[:, dc, 512:1024],
                        start=(dc == 0), stop=(dc == DC - 1))
                tok = pe.inc(pe_i)
                if half == 0:
                    pe.wait(nc.scalar, tok)
                    for ps_, tt in ((bA, 0), (bB, 1)):
                        ai = nc.scalar.activation(
                            out=a_sb[:, og, tt * 512:(tt + 1) * 512],
                            in_=ps_[:, 0:512], func=AF.Silu)
                        st = ac.inc(ai)
                        silu_tok[(og, tt)] = st
                        bk_set(ps_, ac, st)
                else:
                    pe.wait(nc.vector, tok)
                    ac.wait(nc.vector, silu_tok[(og, 1)])
                    if routed:
                        pe.wait(nc.vector, cwb_tok)
                    for ps_, tt in ((bA, 0), (bB, 1)):
                        di = nc.vector.tensor_tensor(
                            out=a_sb[:, og, tt * 512:(tt + 1) * 512],
                            in0=ps_[:, 0:512],
                            in1=a_sb[:, og, tt * 512:(tt + 1) * 512], op=ALU.mult)
                        if routed:
                            di = nc.vector.tensor_tensor(
                                out=a_sb[:, og, tt * 512:(tt + 1) * 512],
                                in0=PS0[:, base + tt * 512:base + (tt + 1) * 512],
                                in1=a_sb[:, og, tt * 512:(tt + 1) * 512], op=ALU.mult)
                        bk_set(ps_, dv, dv.inc(di))
            up_used[gi] = pe.n
            pump_up()
            if not routed:
                drain_gate(21)
        a_ready = dv.n
        if routed:
            cwb_guard[parb] = dv.n
        for og in range(8):
            gi = vi * 8 + og
            gq.wait(nc.tensor, dn_dma_tok[gi])
            for half in range(2):
                oc = 2 * og + half
                bA, bB = next_pair()
                bk_wait(bA)
                bk_wait(bB)
                dv.wait(nc.tensor, a_ready)
                for ic in range(IC):
                    nc.tensor.matmul(
                        bA[:, 0:512],
                        dnring[:, gi % 3, ic, half * 128:(half + 1) * 128],
                        a_sb[:, ic, 0:512],
                        start=(ic == 0), stop=(ic == IC - 1))
                    pe_i = nc.tensor.matmul(
                        bB[:, 0:512],
                        dnring[:, gi % 3, ic, half * 128:(half + 1) * 128],
                        a_sb[:, ic, 512:1024],
                        start=(ic == 0), stop=(ic == IC - 1))
                tok = pe.inc(pe_i)
                pe.wait(nc.vector, tok)
                for ps_, tt in ((bA, 0), (bB, 1)):
                    if vi == 0:
                        di = nc.vector.tensor_copy(acc[:, oc, tt * 512:(tt + 1) * 512],
                                                   ps_[:, 0:512])
                    else:
                        di = nc.vector.tensor_tensor(
                            out=acc[:, oc, tt * 512:(tt + 1) * 512], in0=ps_[:, 0:512],
                            in1=acc[:, oc, tt * 512:(tt + 1) * 512], op=ALU.add)
                    bk_set(ps_, dv, dv.inc(di))
            dn_used[gi] = pe.n
            pump_dn()
            if not routed:
                drain_gate(21)

    if debug:
        for oc in range(DC):
            dv.wait(nc.sync, dv.n)
            nc.sync.dma_start(out=dbg["dbg_acc"][oc], in_=acc[:, oc, :]).then_inc(so, 16)
            so_n[0] += 16

    barB = barrier_tokens()
    esB.close()
    esL2 = ExitStack()
    htm2 = esL2.enter_context(nc.sbuf_tensor("htm2", [128, 2, T], F32))
    sqt2 = esL2.enter_context(nc.sbuf_tensor("sqt2", [128, 2, T], F32))
    out_tm = esL2.enter_context(nc.sbuf_tensor("out_tm", [128, 2, D], F32))

    guard(nc.tensor, barB)
    guard(nc.vector, barB)
    guard(nc.scalar, barB)
    ffn_dv = dv.n
    ffn_ac = ac.n
    h_rd_tok = [0] * DC
    r2_tok = [0] * DC
    for oc in range(DC):
        sl = oc % 2
        if oc == 0:
            guard(nc.sync, barB)
        if oc >= 2:
            dv.wait(nc.sync, r2_tok[oc - 2])
        sq.wait(nc.sync, hsp_tok[oc])
        h_rd_tok[oc] = sq.dma(htm2[:, sl, :], d_hsp[oc])
        sq.wait(nc.vector, h_rd_tok[oc])
        di = nc.vector.tensor_tensor(out=acc[:, oc, :], in0=htm2[:, sl, :],
                                     in1=acc[:, oc, :], op=ALU.add)
        r2_tok[oc] = dv.inc(di)
        dv.wait(nc.scalar, r2_tok[oc])
        ai = nc.scalar.activation(out=sqt2[:, sl, :], in_=acc[:, oc, :], func=AF.Square)
        sq2 = ac.inc(ai)
        dv.wait(nc.tensor, r2_tok[oc])
        ac.wait(nc.tensor, sq2)
        if oc == 0:
            dv.wait(nc.tensor, ffn_dv)
            ac.wait(nc.tensor, ffn_ac)
        for tt in range(NTT):
            nc.tensor.matmul(PS0[0:1, tt * 512:(tt + 1) * 512], ones_f[:, 0:1],
                             acc[:, oc, tt * 512:(tt + 1) * 512],
                             start=(oc == 0), stop=(oc == DC - 1))
            pe_i = nc.tensor.matmul(PS0[0:1, 1024 + tt * 512:1024 + (tt + 1) * 512],
                                    ones_f[:, 0:1], sqt2[:, sl, tt * 512:(tt + 1) * 512],
                                    start=(oc == 0), stop=(oc == DC - 1))
    stats2_tok = pe.inc(pe_i)
    pe.wait(nc.vector, stats2_tok)
    nc.vector.tensor_scalar_mul(mu, PS0[0:1, 0:1024], 1.0 / D)
    nc.vector.tensor_scalar_mul(e2, PS0[0:1, 1024:2048], 1.0 / D)
    nc.vector.tensor_tensor(out=rstd, in0=mu, in1=mu, op=ALU.mult)
    di = nc.vector.tensor_tensor(out=e2, in0=e2, in1=rstd, op=ALU.subtract)
    t_var2 = dv.inc(di)
    dv.wait(nc.scalar, t_var2)
    ai = nc.scalar.activation(out=e2, in_=e2, func=AF.Sqrt, bias=eps_sb[0:1, 0:1])
    t_std2 = ac.inc(ai)
    ac.wait(nc.vector, t_std2)
    di = nc.vector.reciprocal(rstd, e2)
    t_rstd2 = dv.inc(di)
    dv.wait(nc.tensor, t_rstd2)
    bk_wait(PS1)
    for tc in range(8):
        nc.tensor.matmul(PS1[:, 2 * tc:2 * tc + 1], mu[0:1, tc * 128:(tc + 1) * 128],
                         ones_f[0:1, 0:1], start=True, stop=True)
        pe_i = nc.tensor.matmul(PS1[:, 2 * tc + 1:2 * tc + 2],
                                rstd[0:1, tc * 128:(tc + 1) * 128],
                                ones_f[0:1, 0:1], start=True, stop=True)
    t_cols = pe.inc(pe_i)
    pe.wait(nc.vector, t_cols)
    di = nc.vector.tensor_copy(stat_tm[:, :], PS1[:, 0:16])
    t_stm = dv.inc(di)
    nc.vector.tensor_copy(sqt2[:, 0, 0:512], acc[:, 0, 0:512])
    dv.wait(nc.tensor, t_stm)

    otok = [None, None]
    out_toks = []
    for tc in range(8):
        sl = tc % 2
        if otok[sl]:
            sq.wait(nc.vector, otok[sl])
        for r in range(4):
            ps = (PS2, PS3)[r % 2]
            bk_wait(ps)
            for j in range(4):
                oc = 4 * r + j
                pe_i = nc.tensor.transpose(ps[:, j * 128:(j + 1) * 128],
                                           acc[:, oc, tc * 128:(tc + 1) * 128],
                                           ident_f[:, :])
            tok = pe.inc(pe_i)
            pe.wait(nc.vector, tok)
            di = nc.vector.tensor_scalar(out=out_tm[:, sl, r * 512:(r + 1) * 512],
                                         in0=ps[:, 0:512],
                                         scalar1=stat_tm[:, 2 * tc:2 * tc + 1],
                                         scalar2=stat_tm[:, 2 * tc + 1:2 * tc + 2],
                                         op0=ALU.subtract, op1=ALU.mult)
            bk_set(ps, dv, dv.inc(di))
        dv.wait(nc.sync, dv.n)
        otok[sl] = sq.dma(d_out[tc * 128:(tc + 1) * 128, :], out_tm[:, sl, :])
        out_toks.append(otok[sl])
    for t_ in out_toks:
        sq.wait(nc.sync, t_)
    if so_n[0]:
        nc.sync.wait_ge(so, so_n[0])

    esL2.close()
    esH.close()
    esAO.close()
    es0.close()
    return nc


def _prep(inputs):
    f = np.float32
    W = {k: np.asarray(v) for k, v in inputs.items()}
    out = {}

    for nm in ("q", "k", "v"):
        wt = np.asarray(W[f"L0_attn_W{nm}"], f).T
        out[f"w{nm}T"] = np.ascontiguousarray(
            wt.reshape(DC, 128, H, 128).transpose(2, 1, 0, 3))
        out[f"b{nm}"] = np.ascontiguousarray(
            np.asarray(W[f"L0_attn_b{nm}"], f).reshape(H, 128).T)
    wt = np.asarray(W["L0_attn_Wo"], f).T
    out["woT"] = np.ascontiguousarray(wt.reshape(DC, 128, 4, 512).transpose(2, 1, 0, 3))
    out["bo"] = np.ascontiguousarray(np.asarray(W["L0_attn_bo"], f).reshape(H, 128).T)
    Bcat = np.concatenate([np.asarray(W["L0_attn_Bq"], f).T,
                           np.asarray(W["L0_attn_Bk"], f).T,
                           np.asarray(W["L0_attn_Bv"], f).T], axis=1)
    out["BqkvT"] = np.ascontiguousarray(Bcat.reshape(DC, 128, 3 * R).transpose(1, 0, 2))
    out["BoT"] = np.ascontiguousarray(
        np.asarray(W["L0_attn_Bo"], f).T.reshape(DC, 128, R).transpose(1, 0, 2))
    for nm in ("q", "k", "v", "o"):
        out[f"A{nm}T"] = np.ascontiguousarray(np.asarray(W[f"L0_attn_A{nm}"], f).T)
    out["gateT"] = np.ascontiguousarray(
        np.asarray(W["L0_ffn_gate_w"], f).T.reshape(DC, 128, E).transpose(1, 0, 2))

    def up_tiles(uT):
        blocks = []
        for j in range(IC):
            blk = np.concatenate([uT[:, j * 128:(j + 1) * 128],
                                  uT[:, 1408 + j * 128:1408 + (j + 1) * 128]], axis=1)
            blocks.append(blk.reshape(DC, 128, 256).transpose(1, 0, 2))
        return np.stack(blocks, axis=0)

    up = np.asarray(W["L0_ffn_expert_up_w"], f)
    out["upT"] = np.ascontiguousarray(
        np.stack([up_tiles(up[e].T) for e in range(E)], axis=0))
    dn = np.asarray(W["L0_ffn_expert_down_w"], f)
    out["dnT"] = np.ascontiguousarray(
        np.stack([dn[e].T.reshape(IC, 128, 8, 256).transpose(2, 1, 0, 3)
                  for e in range(E)], axis=0))
    sup = np.asarray(W["L0_ffn_shared_up_w"], f).T
    sup_ph = []
    for ph in range(2):
        blocks = []
        for j in range(IC):
            c = IC * ph + j
            blk = np.concatenate([sup[:, c * 128:(c + 1) * 128],
                                  sup[:, 2816 + c * 128:2816 + (c + 1) * 128]], axis=1)
            blocks.append(blk.reshape(DC, 128, 256).transpose(1, 0, 2))
        sup_ph.append(np.stack(blocks, axis=0))
    out["supT"] = np.ascontiguousarray(np.stack(sup_ph, axis=0))
    sdn = np.asarray(W["L0_ffn_shared_down_w"], f).T
    out["sdnT"] = np.ascontiguousarray(
        np.stack([sdn[ph * 1408:(ph + 1) * 1408, :]
                  .reshape(IC, 128, 8, 256).transpose(2, 1, 0, 3)
                  for ph in range(2)], axis=0))
    out["identity"] = np.eye(128, dtype=f)
    out["ones"] = np.ones((128, 128), dtype=f)
    esel = np.zeros((R, E * 128), dtype=f)
    for e in range(E):
        esel[e, e * 128:(e + 1) * 128] = 1.0
    out["esel"] = esel
    out["epsv"] = np.full((128, 1), EPS, dtype=f)

    x = np.asarray(W["L0_attn_x"], f)
    percore = []
    for c in range(8):
        b, s = c // 2, c % 2
        xT = np.concatenate([x[b].T[:, s * T:], x[b].T[:, :s * T]], axis=1)
        percore.append({
            "xT": np.ascontiguousarray(xT.reshape(DC, 128, TF).transpose(1, 0, 2)),
            "xTo": np.ascontiguousarray(
                xT[:, 0:T].reshape(DC, 128, T).transpose(1, 0, 2)),
        })
    return out, percore


def kernel(**inputs):
    shared, percore = _prep(inputs)
    nc = build(debug=False)
    in_maps = []
    for c in range(8):
        m = dict(shared)
        m.update(percore[c])
        in_maps.append(m)
    res = run_bass_kernel_spmd(nc, in_maps, core_ids=list(range(8)))
    outs = [res.results[c]["out"] for c in range(8)]
    return np.stack(outs, axis=0).reshape(4, 2048, 2048).astype(np.float32)
